# version 10
# baseline (speedup 1.0000x reference)
import math
from contextlib import ExitStack

import numpy as np
import concourse.bass as bass
import concourse.mybir as mybir
from concourse.bass_utils import run_bass_kernel_spmd

F32 = mybir.dt.float32
BF16 = mybir.dt.bfloat16
I32 = mybir.dt.int32
AF = mybir.ActivationFunctionType
ALU = mybir.AluOpType
AX = mybir.AxisListType

INV_SQRT_S = 1.0 / math.sqrt(128.0)
EPS = 1e-5

FULL_CFG = dict(B=4, Lb=4096, D=1024, E=2048, S=128, H=4096, WREL=2047,
                n_cores=8, debug=())


def chunks_of(total, width):
    out, off = [], 0
    while off < total:
        w = min(width, total - off)
        out.append((off, w))
        off += w
    return out


tiles_of = lambda total: chunks_of(total, 128)


class Sem:
    def __init__(self, nc, es, name):
        self.h = es.enter_context(nc.semaphore(name))
        self.n = 0


def build_program(cfg):
    nc = bass.Bass("TRN2", target_bir_lowering=False, detect_race_conditions=False)
    es = ExitStack()
    D, E, S, H, Lb, W = cfg["D"], cfg["E"], cfg["S"], cfg["H"], cfg["Lb"], cfg["WREL"]
    DBG = cfg.get("debug", ())
    DC, EC, HC, JT = D // 128, E // 128, H // 128, Lb // 128
    Town = Lb // 2
    Ti = Town + 2
    TH = Town // 2
    halfW = TH + 2
    TT = Lb // 128
    ITL = tiles_of(Ti)
    NIT = len(ITL)
    IC = chunks_of(Ti, 512)
    KSTR = max(w for _, w in IC)
    VAR_JT = min((W + 127) // 128, JT)
    NTT = TH // 128
    hs = S // 2
    QW = min(512, max(H, D))

    dd = {}
    I = lambda n, sh, dt=F32: nc.dram_tensor(n, sh, dt, kind="ExternalInput")
    N = lambda n, sh, dt=F32: nc.dram_tensor(n, sh, dt, kind="Internal")
    dd["x_batch"] = I("x_batch", [Lb, D])
    dd["x_own"] = I("x_own", [Ti, D])
    dd["uvw_u_T"] = I("uvw_u_T", [D, E])
    dd["uvw_v_T"] = I("uvw_v_T", [D, E])
    dd["uvw_b_T"] = I("uvw_b_T", [D, S])
    dd["o_w_T"] = I("o_w_T", [E, D])
    dd["w1_T"] = I("w1_T", [D, H])
    dd["w2_T"] = I("w2_T", [H, D])
    dd["conv_w"] = I("conv_w", [H, 3])
    dd["conv_b"] = I("conv_b", [H, 1])
    dd["gammaT"] = I("gammaT", [S, 2])
    dd["betaT"] = I("betaT", [S, 2])
    dd["g_s"] = I("g_s", [1, 1])
    dd["res_row"] = I("res_row", [1, D])
    dd["w_rel"] = I("w_rel", [1, W])
    dd["m_ext"] = I("m_ext", [1, W + 1])
    dd["halo_m"] = I("halo_m", [1, 2])
    dd["halo_cm"] = I("halo_cm", [128, 2])
    dd["ident"] = I("ident", [128, 128], BF16)
    dd["ident_f"] = I("ident_f", [128, 128], F32)
    dd["cos_k"] = I("cos_k", [hs, Lb])
    dd["sin_k"] = I("sin_k", [hs, Lb])
    dd["cos_q"] = I("cos_q", [hs, Ti])
    dd["sin_q"] = I("sin_q", [hs, Ti])
    dd["rev_ext"] = N("rev_ext", [1, 2 * Lb])
    dd["v_scr"] = N("v_scr", [Lb, E], BF16)
    dd["u_scr"] = N("u_scr", [E, Ti], BF16)
    dd["x1_scr"] = N("x1_scr", [Ti, D])
    dd["w1q_scr"] = N("w1q_scr", [HC, DC * 128 * 128], BF16)
    dd["w2q_scr"] = N("w2q_scr", [H, D], BF16)
    dd["row_scr"] = N("row_scr", [1, 2 * Ti + 256])
    dd["out"] = nc.dram_tensor("out", [Town, D], F32, kind="ExternalOutput")
    dbg_shapes = {"xnT_all": ([128, DC * Lb], BF16), "kT": ([S, Lb], BF16),
                  "qT": ([S, Ti], BF16), "x1": ([Ti, D], F32),
                  "kernall": ([len(chunks_of(Lb // 2 + 2, 512)),
                               128 * (Lb // 128) * max(w for _, w in chunks_of(Lb // 2 + 2, 512))], BF16),
                  "xqT": ([128, DC * Ti], BF16), "kern": ([128, JT * KSTR], BF16),
                  "hq": ([128, HC * TH], BF16), "h": ([128, HC * TH], BF16),
                  "v": ([Lb, E], BF16),
                  "u": ([E, Ti], BF16), "w1q": ([HC, DC * 128 * 128], BF16),
                  "w2q": ([H, D], BF16), "rev": ([1, 2 * Lb], F32),
                  "cc": ([128, 8 * (Lb // 128)], F32)}
    for name in DBG:
        sh, dt = dbg_shapes[name]
        dd["dbg_" + name] = nc.dram_tensor("dbg_" + name, sh, dt, kind="ExternalOutput")

    sb = lambda n, sh, dt=F32: es.enter_context(nc.sbuf_tensor(n, sh, dt))
    ident = sb("identb", [128, 128], BF16)
    ident_f = sb("identf", [128, 128])
    gq_col = sb("gq_col", [S, 1])
    bq_col = sb("bq_col", [S, 1])
    gk_col = sb("gk_col", [S, 1])
    bk_col = sb("bk_col", [S, 1])
    g_col = sb("g_colb", [128, 64])
    c_col = sb("c_colb", [128, 1])
    halo_sb = sb("halo_sb", [1, 2])
    halo_cm = sb("halo_cm_sb", [128, 2])
    res1p = sb("res1p", [128, D])
    onesb = sb("onesb", [128, 1], BF16)
    wm1_col = sb("wm1_col", [128, 1])
    wm2_col = sb("wm2_col", [128, 64])
    ws1_col = sb("ws1_col", [128, 1])
    ws2_col = sb("ws2_col", [128, 1])
    kT = sb("kT", [S, Lb], BF16)
    qT = sb("qT", [S, Ti], BF16)
    xqT = sb("xqT", [128, DC * Ti], BF16)
    b_wT = sb("b_wT", [128, DC * S], BF16)

    sm = lambda n: Sem(nc, es, n)
    ph = sm("ph")
    s_p0 = sm("s_p0")
    s_misc = sm("s_misc")
    s_xl = sm("s_xl")
    s_ra = sm("s_ra")
    s_rv = sm("s_rv")
    s_xn = sm("s_xn")
    s_tp = sm("s_tp")
    s_tpe = sm("s_tpe")
    s_mm = sm("s_mm")
    s_ev = sm("s_ev")
    s_sp = sm("s_sp")
    s_spg = sm("s_spg")
    s_wl = sm("s_wl")
    s_wlg = sm("s_wlg")
    s_smm = sm("s_smm")
    s_sv1 = sm("s_sv1")
    s_sa = sm("s_sa")
    s_sq = sm("s_sq")
    s_amm = sm("s_amm")
    s_aev = sm("s_aev")
    s_pmm = sm("s_pmm")
    s_pv = sm("s_pv")
    s_pa = sm("s_pa")
    s_xq = sm("s_xq")
    s_wqg = sm("s_wqg")
    s_wqa = sm("s_wqa")
    s_wqv = sm("s_wqv")
    s_cmm = sm("s_cmm")
    s_cv = sm("s_cv")
    s_cg = sm("s_cg")
    s_ca = sm("s_ca")
    s_q2 = sm("s_q2")
    s_dmm = sm("s_dmm")
    s_dv = sm("s_dv")
    s_da = sm("s_da")
    s_out = sm("s_out")

    all_sems = [ph, s_p0, s_misc, s_xl, s_ra, s_rv, s_xn, s_tp, s_tpe, s_mm,
                s_ev, s_sp, s_spg, s_wl, s_wlg, s_smm, s_sv1, s_sa, s_sq,
                s_amm, s_aev, s_pmm, s_pv, s_pa, s_xq, s_wqg, s_wqa, s_wqv,
                s_cmm, s_cv, s_cg, s_ca, s_q2, s_dmm, s_dv, s_da, s_out]
    dummyb = sb("dummyb", [1, 8])
    ws = sb("dve_ws", [128, 256])
    wsA, wsB, wsC, wsD = (ws[:, 0:64], ws[:, 64:128],
                          ws[:, 128:192], ws[:, 192:256])

    def dve_gap(e):
        e.tensor_tensor(ws[:, 192:256], ws[:, 128:192], ws[:, 128:192],
                        op=ALU.mult)

    def ln_var_w64(e, cols_t, base, tw):
        e.tensor_scalar(wsA[:tw, :], cols_t[:tw, base:base + 64],
                        1.0 / D, None, op0=ALU.mult)
        e.tensor_scalar(wsB[:tw, :], cols_t[:tw, base + 1:base + 65],
                        1.0 / D, None, op0=ALU.mult)
        e.tensor_tensor(wsC[:tw, :], wsA[:tw, :], wsA[:tw, :], op=ALU.mult)
        e.tensor_scalar(wsD[:tw, :], wsB[:tw, :], EPS, None, op0=ALU.add)
        e.tensor_tensor(wsB[:tw, :], wsD[:tw, :], wsC[:tw, :], op=ALU.subtract)
        dve_gap(e)
        return e.tensor_copy(cols_t[:tw, base + 4:base + 5], wsB[:tw, 0:1])

    def ln_scale_w64(e, cols_t, base, tw):
        e.tensor_scalar(wsB[:tw, :], cols_t[:tw, base + 5:base + 69],
                        1e-12, None, op0=ALU.max)
        e.reciprocal(wsC[:tw, :], wsB[:tw, :])
        e.tensor_tensor(wsD[:tw, :], wsA[:tw, :], wsC[:tw, :], op=ALU.mult)
        e.tensor_scalar(wsB[:tw, :], wsD[:tw, :], -1.0, None, op0=ALU.mult)
        dve_gap(e)
        e.tensor_copy(cols_t[:tw, base + 8:base + 9], wsC[:tw, 0:1])
        return e.tensor_copy(cols_t[:tw, base + 9:base + 10], wsB[:tw, 0:1])

    def amax127_w64(e, cols_t, col_in, col_out, tw, xt):
        e.reduce_max(cols_t[:tw, col_in:col_in + 1], xt, axis=AX.X,
                     apply_absolute_value=True)
        dve_gap(e)
        dve_gap(e)
        e.tensor_scalar(wsA[:tw, :], cols_t[:tw, col_in:col_in + 64],
                        1e-5, None, op0=ALU.max)
        e.reciprocal(wsB[:tw, :], wsA[:tw, :])
        e.tensor_scalar(wsC[:tw, :], wsB[:tw, :], 127.0, None, op0=ALU.mult)
        dve_gap(e)
        return e.tensor_copy(cols_t[:tw, col_out:col_out + 1], wsC[:tw, 0:1])
    sem_nums = sorted(s.h.num for s in all_sems)
    assert sem_nums == list(range(sem_nums[0], sem_nums[0] + len(sem_nums))), sem_nums
    with nc.Block() as blk0:
        @blk0.gpsimd
        def _(e):
            e.sem_clear(range(sem_nums[0], sem_nums[-1] + 1))
        @blk0.vector
        def _(e):
            e.memset(dummyb[:, :], 0.0)

    blk = es.enter_context(nc.Block())
    ENGS = ["sync", "gpsimd", "scalar", "vector", "tensor"]

    def barrier():
        drains = {"sync": [(s_sp, s_sp.n), (s_xl, s_xl.n), (s_out, s_out.n)],
                  "gpsimd": [(s_spg, s_spg.n), (s_wlg, s_wlg.n)],
                  "scalar": [(s_wl, s_wl.n)]}
        for name in ENGS:
            dr = drains.get(name, [])
            def _f(e, dr=dr, name=name):
                for sem, val in dr:
                    if val > 0:
                        e.wait_ge(sem.h, val)
                if name == "scalar":
                    e.activation(dummyb[0:1, 0:1], dummyb[0:1, 1:2],
                                 AF.Identity).then_inc(ph.h, 1)
                elif name == "vector":
                    e.tensor_copy(dummyb[0:1, 2:3], dummyb[0:1, 3:4]).then_inc(ph.h, 1)
                elif name == "gpsimd":
                    e.memset(dummyb[0:1, 4:5], 0.0).then_inc(ph.h, 1)
                else:
                    e.sem_inc(ph.h, 1)
            getattr(blk, name)(_f)
        ph.n += 5
        t = ph.n
        for name in ENGS:
            getattr(blk, name)(lambda e, t=t: e.wait_ge(ph.h, t))

    es_p0 = ExitStack()
    sb0 = lambda n, sh, dt=F32: es_p0.enter_context(nc.sbuf_tensor(n, sh, dt))
    rev_sb = sb0("rev_sb", [1, 2 * Lb])
    wr_sb = sb0("wr_sb", [1, W + 1])
    mext_sb = sb0("mext_sb", [1, W + 1])
    gb_sb = sb0("gb_sb", [S, 4])

    @blk.sync
    def _(e):
        e.dma_start(ident[:, :], dd["ident"][:, :]).then_inc(s_p0.h, 16)
        e.dma_start(ident_f[:, :], dd["ident_f"][:, :]).then_inc(s_p0.h, 16)
        e.dma_start(gb_sb[:, 0:2], dd["gammaT"][:, :]).then_inc(s_p0.h, 16)
        e.dma_start(gb_sb[:, 2:4], dd["betaT"][:, :]).then_inc(s_p0.h, 16)
        e.dma_start(g_col[:, 0:1], bass.AP(dd["g_s"], 0, [[0, 128], [1, 1]])).then_inc(s_p0.h, 16)
        e.dma_start(res1p[:, :], bass.AP(dd["res_row"], 0, [[0, 128], [1, D]])).then_inc(s_p0.h, 16)
        e.dma_start(wr_sb[0:1, 0:1], dd["w_rel"][0:1, 0:1]).then_inc(s_p0.h, 16)
        e.dma_start(wr_sb[0:1, 1:W + 1], dd["w_rel"][0:1, 0:W]).then_inc(s_p0.h, 16)
        e.dma_start(mext_sb[:, :], dd["m_ext"][:, :]).then_inc(s_p0.h, 16)
        e.dma_start(halo_sb[:, :], dd["halo_m"][:, :]).then_inc(s_p0.h, 16)
        e.dma_start(halo_cm[:, :], dd["halo_cm"][:, :]).then_inc(s_p0.h, 16)
    s_p0.n += 11 * 16

    @blk.vector
    def _(e):
        e.wait_ge(s_p0.h, s_p0.n)
        e.memset(ws[:, :], 1.0)
        e.memset(g_col[:, 1:64], 1.0)
        e.memset(wm2_col[:, 1:64], 1.0)
        e.memset(onesb[:, :], 1.0)
        e.tensor_scalar_add(res1p[:, :], res1p[:, :], 1.0)
        e.tensor_scalar(gq_col[:, :], gb_sb[:, 0:1], INV_SQRT_S, None, op0=ALU.mult)
        e.tensor_scalar(bq_col[:, :], gb_sb[:, 2:3], INV_SQRT_S, None, op0=ALU.mult)
        e.tensor_copy(gk_col[:, :], gb_sb[:, 1:2])
        e.tensor_copy(bk_col[:, :], gb_sb[:, 3:4])
        e.tensor_scalar(wr_sb[0:1, :], wr_sb[0:1, :], INV_SQRT_S, None, op0=ALU.mult)
        dve_gap(e)
        dve_gap(e)
        e.tensor_single_scalar(wr_sb[0:1, 0:W], wr_sb[0:1, 0:W], wr_sb[0:1, W:W + 1],
                               op=ALU.subtract)
        e.tensor_tensor(wr_sb[0:1, 0:W], wr_sb[0:1, 0:W], mext_sb[0:1, 0:W], op=ALU.mult)
        e.tensor_single_scalar(wr_sb[0:1, 0:W], wr_sb[0:1, 0:W], wr_sb[0:1, W:W + 1],
                               op=ALU.add)
        e.memset(rev_sb[0:1, :], 0.0)
        dve_gap(e)
        dve_gap(e)
        e.tensor_single_scalar(rev_sb[0:1, :], rev_sb[0:1, :], wr_sb[0:1, W:W + 1],
                               op=ALU.add)
        e.tensor_copy(rev_sb[0:1, Town - 1: Town + W],
                      wr_sb[0:1, 0:W + 1]).then_inc(s_misc.h, 1)
    s_misc.n += 1

    @blk.sync
    def _(e):
        e.wait_ge(s_misc.h, s_misc.n)
        e.dma_start(dd["rev_ext"][:, :], rev_sb[0:1, :]).then_inc(s_sp.h, 16)
        e.dma_start(dd["row_scr"][0:1, 0:1], wr_sb[0:1, W:W + 1]).then_inc(s_sp.h, 16)
        e.wait_ge(s_sp.h, s_sp.n + 32)
        e.dma_start(c_col[:, :], bass.AP(dd["row_scr"], 0, [[0, 128], [1, 1]])).then_inc(s_sp.h, 16)
        e.wait_ge(s_sp.h, s_sp.n + 48)
        if "rev" in DBG:
            e.dma_start(dd["dbg_rev"][:, :], rev_sb[0:1, :]).then_inc(s_sp.h, 16)
    s_sp.n += 48 + (16 if "rev" in DBG else 0)

    barrier()
    es_p0.close()

    def _stop(tag):
        if cfg.get("stop_after") == tag:
            es.close()
            return True
        return False

    if _stop("P0"):
        return nc, dd

    NXB = 4

    def emit_xn_transpose(tiles, xb, xnb, cols, sjunk, xl_base, dstT, dst_stride,
                          pstr, ccdump=None):
        a0, v0, x0 = s_ra.n, s_rv.n, s_xn.n
        tp0, tpe0 = s_tp.n, s_tpe.n
        NG = (DC + 3) // 4
        NT = len(tiles)

        def emit_xn(e, ti):
            t0, tw = tiles[ti]
            cc = cols[:tw, (ti % NXB) * 8:(ti % NXB) * 8 + 8]
            e.wait_ge(s_rv.h, v0 + ti + 1)
            if ti >= NXB:
                e.wait_ge(s_tp.h, tp0 + (ti - NXB + 1) * NG)
            e.activation(xnb[:tw, (ti % NXB) * D:(ti % NXB) * D + D],
                         xb[:tw, (ti % NXB) * D:(ti % NXB) * D + D],
                         AF.Identity, scale=cc[:, 7:8]).then_inc(s_xn.h, 1)

        @blk.scalar
        def _(e):
            for ti, (t0, tw) in enumerate(tiles):
                e.wait_ge(s_xl.h, xl_base + 16 * (ti + 1))
                sl = xb[:tw, (ti % NXB) * D:(ti % NXB) * D + D]
                cc = cols[:tw, (ti % NXB) * 8:(ti % NXB) * 8 + 8]
                e.activation(sjunk[:tw, :], sl, AF.Square,
                             accum_out=cc[:, 0:1]).then_inc(s_ra.h, 1)
                if ti >= 1:
                    emit_xn(e, ti - 1)
                else:
                    e.activation(sjunk[:tw, 0:1], sl[:, 0:1], AF.Identity)
                e.activation(cc[:, 2:3], cc[:, 0:1], AF.Sqrt,
                             scale=1.0 / D).then_inc(s_ra.h, 1)
            emit_xn(e, NT - 1)

        def emit_evac(e, ti, t0, tw):
            for g in range(NG):
                gsn = ti * NG + g
                nch = min(4, DC - g * 4)
                half = (gsn % 2) * 512
                e.wait_ge(s_tp.h, tp0 + gsn + 1)
                dst = bass.AP(dstT, (g * 4) * dst_stride + t0,
                              [[DC * dst_stride, 128], [dst_stride, nch], [1, tw]])
                sap = bass.AP(pstr, half, [[1024, 128], [128, nch], [1, tw]])
                e.tensor_copy(dst, sap).then_inc(s_tpe.h, 1)

        @blk.vector
        def _(e):
            e.memset(cols[:, :], 1.0)
            for ti, (t0, tw) in enumerate(tiles):
                base = (ti % NXB) * 8
                cc = cols[:tw, base:base + 8]
                e.wait_ge(s_ra.h, a0 + 2 * ti + 2)
                e.tensor_scalar(wsA[:tw, :], cols[:tw, base + 2:base + 66],
                                1e-12, None, op0=ALU.max)
                e.reciprocal(wsB[:tw, :], wsA[:tw, :])
                e.tensor_tensor(wsA[:tw, :], wsB[:tw, :], g_col[:tw, :],
                                op=ALU.mult)
                dve_gap(e)
                e.tensor_copy(cc[:, 7:8], wsA[:tw, 0:1]).then_inc(s_rv.h, 1)
                if ccdump is not None:
                    dve_gap(e)
                    e.tensor_copy(ccdump[:tw, ti * 8:(ti + 1) * 8], cc[:, 0:8])
                if ti >= 1:
                    pt0, ptw = tiles[ti - 1]
                    emit_evac(e, ti - 1, pt0, ptw)
            emit_evac(e, NT - 1, tiles[NT - 1][0], tiles[NT - 1][1])

        @blk.tensor
        def _(e):
            for ti, (t0, tw) in enumerate(tiles):
                e.wait_ge(s_xn.h, x0 + ti + 1)
                for g in range(NG):
                    gsn = ti * NG + g
                    if gsn >= 2:
                        e.wait_ge(s_tpe.h, tpe0 + gsn - 1)
                    half = (gsn % 2) * 512
                    nch = min(4, DC - g * 4)
                    for j in range(nch):
                        dc = g * 4 + j
                        mm = e.matmul(pstr[:, half + j * 128: half + j * 128 + tw],
                                      xnb[:tw, (ti % NXB) * D + dc * 128:
                                          (ti % NXB) * D + dc * 128 + 128],
                                      ident[:tw, :tw], start=True, stop=True)
                    mm.then_inc(s_tp.h, 1)

        s_ra.n += 2 * NT
        s_rv.n += NT
        s_xn.n += NT
        s_tp.n += NT * NG
        s_tpe.n += NT * NG

    esA = ExitStack()
    sbA = lambda n, sh, dt=F32: esA.enter_context(nc.sbuf_tensor(n, sh, dt))
    xnT_all = sbA("xnT_all", [128, DC * Lb], BF16)
    esA1 = ExitStack()
    sbA1 = lambda n, sh, dt=F32: esA1.enter_context(nc.sbuf_tensor(n, sh, dt))
    v_wT = sbA1("v_wT", [128, DC * E], BF16)
    xbuf = sbA1("xbuf", [128, NXB * D])
    xnbuf = sbA1("xnbuf", [128, NXB * D], BF16)
    col_a = sbA1("col_a", [128, 8 * NXB + 64])
    sq_junk = sbA1("sq_junk", [128, D])
    vev = sbA1("vev", [128, 2 * 1024], BF16)
    psA_tr = esA.enter_context(nc.psum_tensor("psA_tr", [128, 1024], F32))
    psA_v = esA.enter_context(nc.psum_tensor("psA_v", [128, 2048], F32))
    psA_b = esA.enter_context(nc.psum_tensor("psA_b", [128, 1024], F32))

    @blk.gpsimd
    def _(e):
        for dc in range(DC):
            e.dma_start(v_wT[:, dc * E:(dc + 1) * E],
                        dd["uvw_v_T"][dc * 128:(dc + 1) * 128, :]).then_inc(s_wlg.h, 16)
            e.dma_start(b_wT[:, dc * S:(dc + 1) * S],
                        dd["uvw_b_T"][dc * 128:(dc + 1) * 128, :]).then_inc(s_wlg.h, 16)
    s_wlg.n += 32 * DC
    wlg_vb = s_wlg.n

    btiles = tiles_of(Lb)
    xl0 = s_xl.n
    xn_before_A = s_xn.n

    @blk.sync
    def _(e):
        for ti in range(TT):
            if ti >= NXB:
                e.wait_ge(s_xn.h, xn_before_A + ti - NXB + 1)
            e.dma_start(xbuf[:, (ti % NXB) * D:(ti % NXB + 1) * D],
                        dd["x_batch"][ti * 128:(ti + 1) * 128, :]).then_inc(s_xl.h, 16)
    s_xl.n += 16 * TT

    ccd = sbA1("ccd", [128, 8 * TT]) if "cc" in DBG else None
    emit_xn_transpose(btiles, xbuf, xnbuf, col_a, sq_junk, xl0, xnT_all, Lb,
                      psA_tr, ccdump=ccd)
    if "cc" in DBG:
        @blk.sync
        def _(e):
            e.wait_ge(s_rv.h, s_rv.n)
            e.dma_start(dd["dbg_cc"][:, :], ccd[:, :]).then_inc(s_sp.h, 16)
        s_sp.n += 16
    if _stop("A1x"):
        return nc, dd

    mmv0, evv0, spv0 = s_mm.n, s_ev.n, s_sp.n
    tpeA0 = s_tpe.n - TT * ((DC + 3) // 4)

    NGD = (DC + 3) // 4
    EH = chunks_of(E, 1024)
    NEH = len(EH)

    @blk.tensor
    def _(e):
        e.wait_ge(s_wlg.h, wlg_vb)
        for t in range(TT):
            e.wait_ge(s_tpe.h, tpeA0 + (t + 1) * NGD)
            for ehi, (eh0, ehw) in enumerate(EH):
                k = NEH * t + ehi
                if k >= 2:
                    e.wait_ge(s_ev.h, evv0 + k - 1)
                for dc in range(DC):
                    stat = xnT_all[:, dc * Lb + t * 128: dc * Lb + t * 128 + 128]
                    for (o0, ow) in chunks_of(ehw, 512):
                        mm = e.matmul(psA_v[:, (k % 2) * 1024 + o0:(k % 2) * 1024 + o0 + ow],
                                      stat,
                                      v_wT[:, dc * E + eh0 + o0: dc * E + eh0 + o0 + ow],
                                      start=(dc == 0), stop=(dc == DC - 1))
                mm.then_inc(s_mm.h, 1)
    s_mm.n += NEH * TT

    @blk.scalar
    def _(e):
        for t in range(TT):
            for ehi, (eh0, ehw) in enumerate(EH):
                k = NEH * t + ehi
                e.wait_ge(s_mm.h, mmv0 + k + 1)
                if k >= 2:
                    e.wait_ge(s_sp.h, spv0 + 16 * (k - 1))
                e.activation(vev[:, (k % 2) * 1024:(k % 2) * 1024 + ehw],
                             psA_v[:, (k % 2) * 1024:(k % 2) * 1024 + ehw],
                             AF.Silu).then_inc(s_ev.h, 1)
    s_ev.n += NEH * TT

    @blk.sync
    def _(e):
        for t in range(TT):
            for ehi, (eh0, ehw) in enumerate(EH):
                k = NEH * t + ehi
                e.wait_ge(s_ev.h, evv0 + k + 1)
                e.dma_start(dd["v_scr"][t * 128:(t + 1) * 128, eh0: eh0 + ehw],
                            vev[:, (k % 2) * 1024:(k % 2) * 1024 + ehw]).then_inc(s_sp.h, 16)
    s_sp.n += 16 * NEH * TT

    if _stop("A1v"):
        return nc, dd
    esA1.close()
    cosk = sbA("cosk", [hs, Lb])
    sink = sbA("sink", [hs, Lb])
    base_sb = sbA("base_sb", [S, Lb])
    rope_tmp = sbA("rope_tmp", [hs, Lb])
    bhi_k = sbA("bhi_k", [hs, Lb])

    @blk.scalar
    def _(e):
        e.dma_start(cosk[:, :], dd["cos_k"][:, :]).then_inc(s_wl.h, 16)
        e.dma_start(sink[:, :], dd["sin_k"][:, :]).then_inc(s_wl.h, 16)
    s_wl.n += 32
    wl_ropek = s_wl.n

    KC = chunks_of(Lb, 512)
    mmb0, evb0 = s_mm.n, s_ev.n

    @blk.tensor
    def _(e):
        for tc, (c0, cw) in enumerate(KC):
            if tc >= 2:
                e.wait_ge(s_ev.h, evb0 + tc - 1)
            for dc in range(DC):
                mm = e.matmul(psA_b[:S, (tc % 2) * 512:(tc % 2) * 512 + cw],
                              b_wT[:, dc * S: dc * S + S],
                              xnT_all[:, dc * Lb + c0: dc * Lb + c0 + cw],
                              start=(dc == 0), stop=(dc == DC - 1))
            mm.then_inc(s_mm.h, 1)
    s_mm.n += len(KC)

    @blk.scalar
    def _(e):
        for tc, (c0, cw) in enumerate(KC):
            e.wait_ge(s_mm.h, mmb0 + tc + 1)
            e.activation(base_sb[:, c0:c0 + cw],
                         psA_b[:S, (tc % 2) * 512:(tc % 2) * 512 + cw],
                         AF.Silu).then_inc(s_ev.h, 1)
    s_ev.n += len(KC)

    @blk.vector
    def _(e):
        e.wait_ge(s_ev.h, s_ev.n)
        e.tensor_scalar(base_sb[:, :], base_sb[:, :], gk_col[:, :], None,
                        op0=ALU.mult)
        e.tensor_scalar(base_sb[:, :], base_sb[:, :], bk_col[:, :], None,
                        op0=ALU.add).then_inc(s_misc.h, 1)
    s_misc.n += 1

    @blk.sync
    def _(e):
        e.wait_ge(s_misc.h, s_misc.n)
        e.dma_start(bhi_k[:, :], base_sb[hs:S, :]).then_inc(s_sp.h, 16)
    s_sp.n += 16
    sp_bhi_k = s_sp.n

    @blk.vector
    def _(e):
        e.wait_ge(s_sp.h, sp_bhi_k)
        e.wait_ge(s_wl.h, wl_ropek)
        e.tensor_tensor(rope_tmp[:, :], bhi_k[:, :], sink[:, :], op=ALU.mult)
        e.tensor_tensor(bhi_k[:, :], bhi_k[:, :], cosk[:, :], op=ALU.mult)
        e.tensor_tensor(kT[0:hs, :], base_sb[0:hs, :], cosk[:, :], op=ALU.mult)
        e.tensor_tensor(kT[0:hs, :], kT[0:hs, :], rope_tmp[:, :], op=ALU.subtract)
        e.tensor_tensor(rope_tmp[:, :], base_sb[0:hs, :], sink[:, :], op=ALU.mult)
        e.tensor_tensor(bhi_k[:, :], bhi_k[:, :],
                        rope_tmp[:, :], op=ALU.add).then_inc(s_misc.h, 1)
    s_misc.n += 1

    @blk.gpsimd
    def _(e):
        e.wait_ge(s_misc.h, s_misc.n)
        e.dma_start(kT[hs:S, :], bhi_k[:, :]).then_inc(s_spg.h, 16)
    s_spg.n += 16
    sp_kT_hi = s_spg.n

    @blk.sync
    def _(e):
        if "xnT_all" in DBG:
            e.dma_start(dd["dbg_xnT_all"][:, :], xnT_all[:, :]).then_inc(s_sp.h, 16)
        if "v" in DBG:
            e.wait_ge(s_sp.h, s_sp.n)
            for t in range(TT):
                e.dma_start(dd["dbg_v"][t * 128:(t + 1) * 128, :],
                            dd["v_scr"][t * 128:(t + 1) * 128, :]).then_inc(s_sp.h, 16)
    if "xnT_all" in DBG:
        s_sp.n += 16
    if "v" in DBG:
        s_sp.n += 16 * TT

    barrier()
    esA.close()
    if _stop("A1"):
        return nc, dd

    esA3 = ExitStack()
    sbA3 = lambda n, sh, dt=F32: esA3.enter_context(nc.sbuf_tensor(n, sh, dt))
    xnT_own = sbA3("xnT_own", [128, DC * Ti], BF16)
    u_wT = sbA3("u_wT", [128, DC * E], BF16)
    cosq = sbA3("cosq", [hs, Ti])
    sinq = sbA3("sinq", [hs, Ti])
    xbuf3 = sbA3("xbuf3", [128, NXB * D])
    xnbuf3 = sbA3("xnbuf3", [128, NXB * D], BF16)
    col_a3 = sbA3("col_a3", [128, 8 * NXB + 64])
    sq_junk3 = sbA3("sq_junk3", [128, D])
    uev = sbA3("uev", [128, 2 * 1024], BF16)
    baseq = sbA3("baseq", [S, Ti])
    ropeq = sbA3("ropeq", [hs, Ti])
    bhi_q = sbA3("bhi_q", [hs, Ti])
    wmean_b = sbA3("wmean_b", [128, 2 * QW])
    wm_acc = sbA3("wm_acc", [128, 8])
    wmc_sb = sbA3("wmc_sb", [128, 4], BF16)
    wmrow = sbA3("wmrow", [1, 4])
    psO_tr = esA3.enter_context(nc.psum_tensor("psO_tr", [128, 1024], F32))
    psO_u = esA3.enter_context(nc.psum_tensor("psO_u", [128, 2048], F32))
    psO_b = esA3.enter_context(nc.psum_tensor("psO_b", [128, 1024], F32))

    @blk.gpsimd
    def _(e):
        for dc in range(DC):
            e.dma_start(u_wT[:, dc * E:(dc + 1) * E],
                        dd["uvw_u_T"][dc * 128:(dc + 1) * 128, :]).then_inc(s_wlg.h, 16)
    s_wlg.n += 16 * DC
    wlg_u = s_wlg.n

    @blk.scalar
    def _(e):
        e.dma_start(cosq[:, :], dd["cos_q"][:, :]).then_inc(s_wl.h, 16)
        e.dma_start(sinq[:, :], dd["sin_q"][:, :]).then_inc(s_wl.h, 16)
    s_wl.n += 32
    wl_ropeq = s_wl.n

    xl3 = s_xl.n
    xn_before_A3 = s_xn.n

    @blk.sync
    def _(e):
        for ti, (t0, tw) in enumerate(ITL):
            if ti >= NXB:
                e.wait_ge(s_xn.h, xn_before_A3 + ti - NXB + 1)
            e.dma_start(xbuf3[:tw, (ti % NXB) * D:(ti % NXB) * D + D],
                        dd["x_own"][t0:t0 + tw, :]).then_inc(s_xl.h, 16)
    s_xl.n += 16 * NIT

    emit_xn_transpose(ITL, xbuf3, xnbuf3, col_a3, sq_junk3, xl3, xnT_own, Ti, psO_tr)

    QC = chunks_of(Ti, 512)
    mmq0, evq0 = s_mm.n, s_ev.n
    tpeQ = s_tpe.n

    @blk.tensor
    def _(e):
        e.wait_ge(s_tpe.h, tpeQ)
        for qi, (q0, qw) in enumerate(QC):
            if qi >= 2:
                e.wait_ge(s_ev.h, evq0 + qi - 1)
            for dc in range(DC):
                mm = e.matmul(psO_b[:S, (qi % 2) * 512:(qi % 2) * 512 + qw],
                              b_wT[:, dc * S: dc * S + S],
                              xnT_own[:, dc * Ti + q0: dc * Ti + q0 + qw],
                              start=(dc == 0), stop=(dc == DC - 1))
            mm.then_inc(s_mm.h, 1)
    s_mm.n += len(QC)

    @blk.scalar
    def _(e):
        for qi, (q0, qw) in enumerate(QC):
            e.wait_ge(s_mm.h, mmq0 + qi + 1)
            e.activation(baseq[:, q0:q0 + qw],
                         psO_b[:S, (qi % 2) * 512:(qi % 2) * 512 + qw],
                         AF.Silu).then_inc(s_ev.h, 1)
    s_ev.n += len(QC)

    @blk.vector
    def _(e):
        e.wait_ge(s_ev.h, s_ev.n)
        e.tensor_scalar(baseq[:, :], baseq[:, :], gq_col[:, :], None,
                        op0=ALU.mult)
        e.tensor_scalar(baseq[:, :], baseq[:, :], bq_col[:, :], None,
                        op0=ALU.add).then_inc(s_misc.h, 1)
    s_misc.n += 1

    @blk.sync
    def _(e):
        e.wait_ge(s_misc.h, s_misc.n)
        e.dma_start(bhi_q[:, :], baseq[hs:S, :]).then_inc(s_sp.h, 16)
    s_sp.n += 16
    sp_bhi_q = s_sp.n

    @blk.vector
    def _(e):
        e.wait_ge(s_sp.h, sp_bhi_q)
        e.wait_ge(s_wl.h, wl_ropeq)
        e.tensor_tensor(ropeq[:, :], bhi_q[:, :], sinq[:, :], op=ALU.mult)
        e.tensor_tensor(bhi_q[:, :], bhi_q[:, :], cosq[:, :], op=ALU.mult)
        e.tensor_tensor(qT[0:hs, :], baseq[0:hs, :], cosq[:, :], op=ALU.mult)
        e.tensor_tensor(qT[0:hs, :], qT[0:hs, :], ropeq[:, :], op=ALU.subtract)
        e.tensor_tensor(ropeq[:, :], baseq[0:hs, :], sinq[:, :], op=ALU.mult)
        e.tensor_tensor(bhi_q[:, :], bhi_q[:, :],
                        ropeq[:, :], op=ALU.add).then_inc(s_misc.h, 1)
    s_misc.n += 1

    @blk.gpsimd
    def _(e):
        e.wait_ge(s_misc.h, s_misc.n)
        e.dma_start(qT[hs:S, :], bhi_q[:, :]).then_inc(s_spg.h, 16)
    s_spg.n += 16
    misc_qk_ready = s_misc.n
    sp_qk_hi = s_spg.n

    UC = chunks_of(Ti, 1024)
    NU = EC * len(UC)
    mmu0, evu0, spu0 = s_mm.n, s_ev.n, s_spg.n

    @blk.tensor
    def _(e):
        e.wait_ge(s_wlg.h, wlg_u)
        k = 0
        for et in range(EC):
            for (u0, uw) in UC:
                if k >= 2:
                    e.wait_ge(s_ev.h, evu0 + k - 1)
                for dc in range(DC):
                    stat = u_wT[:, dc * E + et * 128: dc * E + et * 128 + 128]
                    for (o0, ow) in chunks_of(uw, 512):
                        mm = e.matmul(psO_u[:, (k % 2) * 1024 + o0:(k % 2) * 1024 + o0 + ow],
                                      stat,
                                      xnT_own[:, dc * Ti + u0 + o0: dc * Ti + u0 + o0 + ow],
                                      start=(dc == 0), stop=(dc == DC - 1))
                mm.then_inc(s_mm.h, 1)
                k += 1
    s_mm.n += NU

    @blk.scalar
    def _(e):
        k = 0
        for et in range(EC):
            for (u0, uw) in UC:
                e.wait_ge(s_mm.h, mmu0 + k + 1)
                if k >= 2:
                    e.wait_ge(s_spg.h, spu0 + 16 * (k - 1))
                e.activation(uev[:, (k % 2) * 1024:(k % 2) * 1024 + uw],
                             psO_u[:, (k % 2) * 1024:(k % 2) * 1024 + uw],
                             AF.Silu).then_inc(s_ev.h, 1)
                k += 1
    s_ev.n += NU

    @blk.gpsimd
    def _(e):
        k = 0
        for et in range(EC):
            for (u0, uw) in UC:
                e.wait_ge(s_ev.h, evu0 + k + 1)
                e.dma_start(dd["u_scr"][et * 128:(et + 1) * 128, u0:u0 + uw],
                            uev[:, (k % 2) * 1024:(k % 2) * 1024 + uw]).then_inc(s_spg.h, 16)
                k += 1
    s_spg.n += 16 * NU

    def mean_pieces(nrows, ncols):
        cw = min(QW, ncols)
        return [(r0, rw, c0, cwid) for (r0, rw) in tiles_of(nrows)
                for (c0, cwid) in chunks_of(ncols, cw)]

    w1p = mean_pieces(D, H)
    w2p = mean_pieces(H, D)
    HW2 = QW
    wl_m0 = s_wl.n
    wq0v = s_rv.n

    @blk.scalar
    def _(e):
        for i, (r0, rw, c0, cw) in enumerate(w1p):
            if i >= 2:
                e.wait_ge(s_rv.h, wq0v + i - 1)
            e.dma_start(wmean_b[:rw, (i % 2) * HW2:(i % 2) * HW2 + cw],
                        dd["w1_T"][r0:r0 + rw, c0:c0 + cw]).then_inc(s_wl.h, 16)
    s_wl.n += 16 * len(w1p)

    def emit_mean_reduce(e, pieces, wl_base, ac):
        NP = len(pieces)

        def acc(j):
            jr = pieces[j][1]
            e.tensor_tensor(wm_acc[:jr, ac:ac + 1], wm_acc[:jr, ac:ac + 1],
                            wm_acc[:jr, 2 + j % 4:3 + j % 4], op=ALU.add)

        for i, (r0, rw, c0, cw) in enumerate(pieces):
            e.wait_ge(s_wl.h, wl_base + 16 * (i + 1))
            e.reduce_sum(wm_acc[:rw, 2 + i % 4:3 + i % 4],
                         wmean_b[:rw, (i % 2) * HW2:(i % 2) * HW2 + cw],
                         axis=AX.X, apply_absolute_value=True).then_inc(s_rv.h, 1)
            if i >= 2:
                acc(i - 2)
        for j in range(max(0, NP - 2), NP):
            dve_gap(e)
            acc(j)
        dve_gap(e)

    @blk.vector
    def _(e):
        e.memset(wm_acc[:, 0:2], 0.0)
        emit_mean_reduce(e, w1p, wl_m0, 0)
    s_rv.n += len(w1p)

    wl_m1 = s_wl.n
    wq1v = s_rv.n

    @blk.scalar
    def _(e):
        for i, (r0, rw, c0, cw) in enumerate(w2p):
            if i >= 2:
                e.wait_ge(s_rv.h, wq1v + i - 1)
            else:
                e.wait_ge(s_rv.h, wq1v)
            e.dma_start(wmean_b[:rw, (i % 2) * HW2:(i % 2) * HW2 + cw],
                        dd["w2_T"][r0:r0 + rw, c0:c0 + cw]).then_inc(s_wl.h, 16)
    s_wl.n += 16 * len(w2p)

    @blk.vector
    def _(e):
        emit_mean_reduce(e, w2p, wl_m1, 1)
        e.tensor_copy(wmc_sb[:, 0:2], wm_acc[:, 0:2]).then_inc(s_misc.h, 1)
    s_rv.n += len(w2p)
    s_misc.n += 1

    @blk.tensor
    def _(e):
        e.wait_ge(s_misc.h, s_misc.n)
        e.matmul(psO_b[0:1, 0:2], onesb[:, :], wmc_sb[:, 0:2],
                 start=True, stop=True).then_inc(s_mm.h, 1)
    s_mm.n += 1

    @blk.vector
    def _(e):
        e.wait_ge(s_mm.h, s_mm.n)
        e.tensor_scalar(wsA[0:1, :], psO_b[0:1, 0:64], 1.0 / (D * H), 1e-5,
                        op0=ALU.mult, op1=ALU.max)
        e.reciprocal(wsB[0:1, :], wsA[0:1, :])
        dve_gap(e)
        e.tensor_copy(wmrow[0:1, 0:2], wsA[0:1, 0:2])
        e.tensor_copy(wmrow[0:1, 2:4], wsB[0:1, 0:2]).then_inc(s_misc.h, 1)
    s_misc.n += 1

    @blk.sync
    def _(e):
        e.wait_ge(s_misc.h, s_misc.n)
        e.dma_start(dd["row_scr"][0:1, 8:12], wmrow[0:1, 0:4]).then_inc(s_sp.h, 16)
        e.wait_ge(s_sp.h, s_sp.n + 16)
        e.dma_start(wm1_col[:, :], bass.AP(dd["row_scr"], 8, [[0, 128], [1, 1]])).then_inc(s_sp.h, 16)
        e.dma_start(wm2_col[:, 0:1], bass.AP(dd["row_scr"], 9, [[0, 128], [1, 1]])).then_inc(s_sp.h, 16)
        e.dma_start(ws1_col[:, :], bass.AP(dd["row_scr"], 10, [[0, 128], [1, 1]])).then_inc(s_sp.h, 16)
        e.dma_start(ws2_col[:, :], bass.AP(dd["row_scr"], 11, [[0, 128], [1, 1]])).then_inc(s_sp.h, 16)
        e.wait_ge(s_sp.h, s_sp.n + 80)
    s_sp.n += 80

    @blk.sync
    def _(e):
        if "u" in DBG:
            e.wait_ge(s_spg.h, s_spg.n)
            for et in range(EC):
                e.dma_start(dd["dbg_u"][et * 128:(et + 1) * 128, :],
                            dd["u_scr"][et * 128:(et + 1) * 128, :]).then_inc(s_sp.h, 16)
    if "u" in DBG:
        s_sp.n += 16 * EC

    barrier()
    esA3.close()
    if _stop("A3"):
        return nc, dd

    esPB = ExitStack()
    sbB = lambda n, sh, dt=F32: esPB.enter_context(nc.sbuf_tensor(n, sh, dt))
    o_wT = sbB("o_wT", [128, EC * D], BF16)
    kern = sbB("kern", [128, JT * KSTR], BF16)
    outT = sbB("outT", [128, EC * KSTR], BF16)
    bias_b = sbB("bias_b", [128, 2 * KSTR])
    sadd = sbB("sadd", [128, 2 * KSTR])
    srel = sbB("srel", [128, 2 * KSTR])
    vstrip = sbB("vstrip", [128, 2 * JT * 128], BF16)
    ustrip = sbB("ustrip", [128, 2 * KSTR], BF16)
    xres = sbB("xres", [128, 2 * D])
    ytile = sbB("ytile", [128, 2 * D])
    x1t = sbB("x1t", [128, 2 * D])
    xqtile = sbB("xqtile", [128, 2 * D], BF16)
    qtmpB = sbB("qtmpB", [128, D])
    junkB = sbB("junkB", [128, D])
    colB = sbB("colB", [128, 16 * 2 + 64])
    colB_init = [False]
    wqbuf = sbB("wqbuf", [128, 2 * QW])
    wqout = sbB("wqout", [128, 2 * QW], BF16)
    wqi32 = esPB.enter_context(nc.sbuf_tensor("wqi32", [128, QW], I32))
    psB_s = esPB.enter_context(nc.psum_tensor("psB_s", [128, 2048], F32))
    psB_o = esPB.enter_context(nc.psum_tensor("psB_o", [128, 1024], F32))
    psB_p = esPB.enter_context(nc.psum_tensor("psB_p", [128, 1024], F32))

    @blk.gpsimd
    def _(e):
        for ec in range(EC):
            e.dma_start(o_wT[:, ec * D:(ec + 1) * D],
                        dd["o_w_T"][ec * 128:(ec + 1) * 128, :]).then_inc(s_wlg.h, 16)
    s_wlg.n += 16 * EC
    wlg_ow = s_wlg.n

    def emit_wquant(pieces, src, ws_col, sw, store_fn):
        ld0, g0, sp0 = s_wl.n, s_wqg.n, s_spg.n
        wq_g0 = g0
        wq_a0 = s_wqa.n
        wq_v0 = s_wqv.n
        NP = len(pieces)

        @blk.scalar
        def _(e):
            for i in range(NP + 2):
                j = i - 2
                if 0 <= j < NP:
                    r0, rw, c0, cw = pieces[j]
                    sl = wqbuf[:rw, (j % 2) * sw:(j % 2) * sw + cw]
                    e.wait_ge(s_wqg.h, wq_g0 + j + 1)
                    if j >= 1:
                        e.wait_ge(s_wqv.h, wq_v0 + j)
                    e.activation(wqi32[:rw, 0:cw], sl, AF.Identity).then_inc(s_wqa.h, 1)
                if i < NP:
                    r0, rw, c0, cw = pieces[i]
                    e.dma_start(wqbuf[:rw, (i % 2) * sw:(i % 2) * sw + cw],
                                src[r0:r0 + rw, c0:c0 + cw]).then_inc(s_wl.h, 16)

        s_wl.n += 16 * NP

        @blk.gpsimd
        def _(e):
            for i, (r0, rw, c0, cw) in enumerate(pieces):
                sl = wqbuf[:rw, (i % 2) * sw:(i % 2) * sw + cw]
                e.wait_ge(s_wl.h, ld0 + 16 * (i + 1))
                if i >= 2:
                    e.wait_ge(s_wqa.h, wq_a0 + i - 1)
                e.tensor_scalar(sl, sl, ws_col[:rw, :], None,
                                op0=ALU.mult).then_inc(s_wqg.h, 1)
                if i >= 1:
                    p0, pw, q0, qw_ = pieces[i - 1]
                    e.wait_ge(s_wqv.h, wq_v0 + i)
                    store_fn(e, p0, pw, q0, qw_,
                             wqout[:pw, ((i - 1) % 2) * sw:((i - 1) % 2) * sw + qw_])
            p0, pw, q0, qw_ = pieces[-1]
            e.wait_ge(s_wqv.h, wq_v0 + NP)
            store_fn(e, p0, pw, q0, qw_,
                     wqout[:pw, ((NP - 1) % 2) * sw:((NP - 1) % 2) * sw + qw_])

        @blk.vector
        def _(e):
            for i, (r0, rw, c0, cw) in enumerate(pieces):
                ot = wqout[:rw, (i % 2) * sw:(i % 2) * sw + cw]
                e.wait_ge(s_wqa.h, wq_a0 + i + 1)
                if i >= 2:
                    e.wait_ge(s_spg.h, sp0 + 16 * (i - 1))
                e.tensor_copy(ot, wqi32[:rw, 0:cw])
                e.tensor_scalar(ot, ot, -1.0, 1.0,
                                op0=ALU.max, op1=ALU.min).then_inc(s_wqv.h, 1)

        s_wqg.n += len(pieces)
        s_wqa.n += len(pieces)
        s_wqv.n += len(pieces)
        s_spg.n += 16 * len(pieces)

    def store_w1(e, r0, rw, c0, cw, ot):
        dc = r0 // 128
        cs0 = c0 // 128
        ncs = cw // 128
        dst = bass.AP(dd["w1q_scr"], cs0 * DC * 128 * 128 + dc * 128 * 128,
                      [[128, rw], [DC * 128 * 128, ncs], [1, 128]])
        e.dma_start(dst, ot).then_inc(s_spg.h, 16)

    def store_w2(e, r0, rw, c0, cw, ot):
        e.dma_start(dd["w2q_scr"][r0:r0 + rw, c0:c0 + cw], ot).then_inc(s_spg.h, 16)

    amax_off = 16
    inv2_off = 16 + Ti + 16

    for ici, (i0, iw) in enumerate(IC):
        NSC = chunks_of(iw, 512)
        smm0, sv10, sa0, sq0, bias0 = s_smm.n, s_sv1.n, s_sa.n, s_sq.n, s_sp.n
        tpe_prev = s_tpe.n

        @blk.sync
        def _(e, i0=i0, iw=iw, sv10=sv10):
            for jt in range(VAR_JT):
                if jt >= 2:
                    e.wait_ge(s_sv1.h, sv10 + jt - 1)
                elif sv10 > 0:
                    e.wait_ge(s_sv1.h, sv10)
                off = Lb + jt * 128 - i0 - iw + 1
                e.dma_start(bias_b[:, (jt % 2) * KSTR:(jt % 2) * KSTR + iw],
                            bass.AP(dd["rev_ext"], off, [[1, 128], [1, iw]])).then_inc(s_sp.h, 16)
        s_sp.n += 16 * VAR_JT

        @blk.tensor
        def _(e, i0=i0, iw=iw, NSC=NSC, sv10=sv10, sa0=sa0, tpe_prev=tpe_prev):
            e.wait_ge(s_misc.h, misc_qk_ready)
            e.wait_ge(s_spg.h, sp_qk_hi)
            e.wait_ge(s_spg.h, sp_kT_hi)
            e.wait_ge(s_tpe.h, tpe_prev)
            for jt in range(JT):
                if jt >= 2:
                    prev = jt - 2
                    if prev < VAR_JT:
                        e.wait_ge(s_sv1.h, sv10 + prev + 1)
                    else:
                        e.wait_ge(s_sa.h, sa0 + prev + 1)
                else:
                    if sv10 > 0:
                        e.wait_ge(s_sv1.h, sv10)
                    if sa0 > 0:
                        e.wait_ge(s_sa.h, sa0)
                stat = kT[:, jt * 128:(jt + 1) * 128]
                for (o0, ow) in NSC:
                    mm = e.matmul(psB_s[:, (jt % 2) * 1024 + o0:(jt % 2) * 1024 + o0 + ow],
                                  stat, qT[:, i0 + o0:i0 + o0 + ow],
                                  start=True, stop=True)
                mm.then_inc(s_smm.h, 1)
        s_smm.n += JT

        @blk.vector
        def _(e, iw=iw, smm0=smm0, bias0=bias0, sa0=sa0, sq0=sq0):
            for jt in range(JT + 1):
                if jt < JT and jt < VAR_JT:
                    e.wait_ge(s_smm.h, smm0 + jt + 1)
                    e.wait_ge(s_sp.h, bias0 + 16 * (jt + 1))
                    if jt >= 2:
                        e.wait_ge(s_sa.h, sa0 + jt - 1)
                    brev = bass.AP(bias_b, (jt % 2) * KSTR + iw - 1,
                                   [[2 * KSTR, 128], [-1, iw]])
                    e.tensor_tensor(sadd[:, (jt % 2) * KSTR:(jt % 2) * KSTR + iw],
                                    psB_s[:, (jt % 2) * 1024:(jt % 2) * 1024 + iw],
                                    brev, op=ALU.add).then_inc(s_sv1.h, 1)
                if jt >= 1:
                    pj = jt - 1
                    e.wait_ge(s_sa.h, sa0 + pj + 1)
                    e.tensor_tensor(kern[:, pj * KSTR: pj * KSTR + iw],
                                    srel[:, (pj % 2) * KSTR:(pj % 2) * KSTR + iw],
                                    srel[:, (pj % 2) * KSTR:(pj % 2) * KSTR + iw],
                                    op=ALU.mult).then_inc(s_sq.h, 1)
        s_sv1.n += VAR_JT
        s_sq.n += JT

        @blk.scalar
        def _(e, iw=iw, smm0=smm0, sv10=sv10, sq0=sq0):
            for jt in range(JT):
                if jt >= 2:
                    e.wait_ge(s_sq.h, sq0 + jt - 1)
                elif sq0 > 0:
                    e.wait_ge(s_sq.h, sq0)
                if jt < VAR_JT:
                    e.wait_ge(s_sv1.h, sv10 + jt + 1)
                    e.activation(srel[:, (jt % 2) * KSTR:(jt % 2) * KSTR + iw],
                                 sadd[:, (jt % 2) * KSTR:(jt % 2) * KSTR + iw],
                                 AF.Relu).then_inc(s_sa.h, 1)
                else:
                    e.wait_ge(s_smm.h, smm0 + jt + 1)
                    e.activation(srel[:, (jt % 2) * KSTR:(jt % 2) * KSTR + iw],
                                 psB_s[:, (jt % 2) * 1024:(jt % 2) * 1024 + iw],
                                 AF.Relu, bias=c_col[:, :]).then_inc(s_sa.h, 1)
        s_sa.n += JT

        if "kernall" in DBG:
            @blk.sync
            def _(e, ici=ici, sq0=sq0):
                e.wait_ge(s_sq.h, sq0 + JT)
                e.dma_start(dd["dbg_kernall"][ici:ici + 1, :],
                            kern[:, :]).then_inc(s_sp.h, 16)
            s_sp.n += 16

        njt_g = (JT + 3) // 4
        grps = [(g * njt_g, min(njt_g, JT - g * njt_g)) for g in range(4) if g * njt_g < JT]
        vt0, ut0, amm0, aev0 = s_sp.n, s_wl.n, s_amm.n, s_aev.n

        @blk.sync
        def _(e, aev0=aev0, grps=grps):
            for et in range(EC):
                if et >= 2:
                    e.wait_ge(s_aev.h, aev0 + et - 1)
                elif aev0 > 0:
                    e.wait_ge(s_aev.h, aev0)
                for (j0, nj) in grps:
                    src = bass.AP(dd["v_scr"], (j0 * 128) * E + et * 128,
                                  [[E, 128], [128 * E, nj], [1, 128]])
                    e.dma_start(vstrip[:, (et % 2) * JT * 128 + j0 * 128:
                                       (et % 2) * JT * 128 + (j0 + nj) * 128],
                                src).then_inc(s_sp.h, 16)
        s_sp.n += 16 * EC * len(grps)

        @blk.scalar
        def _(e, i0=i0, iw=iw, aev0=aev0):
            for et in range(EC):
                if et >= 2:
                    e.wait_ge(s_aev.h, aev0 + et - 1)
                elif aev0 > 0:
                    e.wait_ge(s_aev.h, aev0)
                e.dma_start(ustrip[:, (et % 2) * KSTR:(et % 2) * KSTR + iw],
                            dd["u_scr"][et * 128:(et + 1) * 128, i0:i0 + iw]).then_inc(s_wl.h, 16)
        s_wl.n += 16 * EC

        @blk.tensor
        def _(e, iw=iw, NSC=NSC, vt0=vt0, sq0=sq0, aev0=aev0, ng=len(grps)):
            e.wait_ge(s_sq.h, sq0 + JT)
            e.wait_ge(s_wlg.h, wlg_ow)
            for et in range(EC):
                e.wait_ge(s_sp.h, vt0 + 16 * ng * (et + 1))
                if et >= 1:
                    e.wait_ge(s_aev.h, aev0 + et)
                for jt in range(JT):
                    stat = vstrip[:, (et % 2) * JT * 128 + jt * 128:
                                  (et % 2) * JT * 128 + jt * 128 + 128]
                    for (o0, ow) in NSC:
                        mm = e.matmul(psB_o[:, o0:o0 + ow], stat,
                                      kern[:, jt * KSTR + o0: jt * KSTR + o0 + ow],
                                      start=(jt == 0), stop=(jt == JT - 1))
                mm.then_inc(s_amm.h, 1)
        s_amm.n += EC

        @blk.vector
        def _(e, iw=iw, amm0=amm0, ut0=ut0):
            for et in range(EC):
                e.wait_ge(s_amm.h, amm0 + et + 1)
                e.wait_ge(s_wl.h, ut0 + 16 * (et + 1))
                e.tensor_tensor(outT[:, et * KSTR: et * KSTR + iw],
                                psB_o[:, 0:iw],
                                ustrip[:, (et % 2) * KSTR:(et % 2) * KSTR + iw],
                                op=ALU.mult).then_inc(s_aev.h, 1)
        s_aev.n += EC
        aev_all = s_aev.n

        tiles_here = [(ti, t0, tw) for ti, (t0, tw) in enumerate(ITL)
                      if t0 >= i0 and t0 + tw <= i0 + iw]
        for (ti, t0, tw) in tiles_here:
            il = t0 - i0
            pmm0, pv0, pa0, xq0 = s_pmm.n, s_pv.n, s_pa.n, s_xq.n
            xr0 = s_sp.n

            @blk.sync
            def _(e, t0=t0, tw=tw, ti=ti, pv0=pv0):
                if pv0 >= 8:
                    e.wait_ge(s_pv.h, pv0 - 7)
                e.dma_start(xres[:tw, (ti % 2) * D:(ti % 2) * D + D],
                            dd["x_own"][t0:t0 + tw, :]).then_inc(s_sp.h, 16)
            s_sp.n += 16

            @blk.tensor
            def _(e, il=il, tw=tw, pv0=pv0):
                e.wait_ge(s_aev.h, aev_all)
                if pv0 > 0:
                    e.wait_ge(s_pv.h, pv0 - 3)
                for (do0, dow) in chunks_of(D, 512):
                    for ek in range(EC):
                        mm = e.matmul(psB_p[:tw, do0: do0 + dow],
                                      outT[:, ek * KSTR + il: ek * KSTR + il + tw],
                                      o_wT[:, ek * D + do0: ek * D + do0 + dow],
                                      start=(ek == 0), stop=(ek == EC - 1))
                mm.then_inc(s_pmm.h, 1)
            s_pmm.n += 1

            @blk.vector
            def _(e, tw=tw, ti=ti, pmm0=pmm0, pa0=pa0, xr0=xr0,
                  init=not colB_init[0]):
                if init:
                    e.memset(colB[:, :], 1.0)
                base = (ti % 2) * 16
                e.wait_ge(s_pmm.h, pmm0 + 1)
                e.wait_ge(s_sp.h, xr0 + 16)
                yt = ytile[:tw, (ti % 2) * D:(ti % 2) * D + D]
                e.tensor_tensor(yt, xres[:tw, (ti % 2) * D:(ti % 2) * D + D],
                                res1p[:tw, :], op=ALU.mult)
                e.tensor_tensor(yt, yt, psB_p[:tw, 0:D], op=ALU.add).then_inc(s_pv.h, 1)
                e.wait_ge(s_pa.h, pa0 + 1)
                ln_var_w64(e, colB, base, tw).then_inc(s_pv.h, 1)
                e.wait_ge(s_pa.h, pa0 + 2)
                ln_scale_w64(e, colB, base, tw).then_inc(s_pv.h, 1)
            colB_init[0] = True
            s_pv.n += 3

            @blk.scalar
            def _(e, tw=tw, ti=ti, pv0=pv0):
                c = colB[:tw, (ti % 2) * 16:(ti % 2) * 16 + 16]
                yt = ytile[:tw, (ti % 2) * D:(ti % 2) * D + D]
                e.wait_ge(s_pv.h, pv0 + 1)
                e.activation(srel.bitcast(F32)[:tw, 0:D] if False else srel[:tw, 0:D],
                             yt, AF.Identity, accum_out=c[:, 0:1])
                e.activation(srel[:tw, 0:D], yt, AF.Square,
                             accum_out=c[:, 1:2]).then_inc(s_pa.h, 1)
                e.wait_ge(s_pv.h, pv0 + 2)
                e.activation(c[:, 5:6], c[:, 4:5], AF.Sqrt).then_inc(s_pa.h, 1)
                e.wait_ge(s_pv.h, pv0 + 3)
                e.activation(x1t[:tw, (ti % 2) * D:(ti % 2) * D + D], yt,
                             AF.Identity, scale=c[:, 8:9], bias=c[:, 9:10]).then_inc(s_pa.h, 1)
            s_pa.n += 3

            @blk.vector
            def _(e, tw=tw, ti=ti, pa0=pa0):
                base = (ti % 2) * 16
                xt = x1t[:tw, (ti % 2) * D:(ti % 2) * D + D]
                e.wait_ge(s_pa.h, pa0 + 3)
                amax127_w64(e, colB, base + 10, base + 11, tw,
                            xt).then_inc(s_pv.h, 1)
            s_pv.n += 1

            @blk.scalar
            def _(e, tw=tw, ti=ti, pv0=pv0, xq0=xq0):
                c = colB[:tw, (ti % 2) * 16:(ti % 2) * 16 + 16]
                xt = x1t[:tw, (ti % 2) * D:(ti % 2) * D + D]
                e.wait_ge(s_pv.h, pv0 + 4)
                if xq0 > 0:
                    e.wait_ge(s_xq.h, xq0)
                e.activation(qtmpB[:tw, 0:D], xt,
                             AF.Identity, scale=c[:, 11:12]).then_inc(s_pa.h, 1)
            s_pa.n += 1

            @blk.vector
            def _(e, tw=tw, ti=ti, pa0=pa0, t0=t0):
                yq = qtmpB[:tw, 0:D]
                xq = xqtile[:tw, (ti % 2) * D:(ti % 2) * D + D]
                jb = junkB.bitcast(I32)[:tw, 0:D]
                e.wait_ge(s_pa.h, pa0 + 4)
                e.tensor_copy(jb, yq)
                e.tensor_copy(xq, jb)
                last = e.tensor_scalar(xq, xq, -128.0, 127.0,
                                       op0=ALU.max, op1=ALU.min)
                if t0 == 0:
                    last = e.tensor_scalar(xq, xq, halo_cm[:tw, 0:1], None,
                                           op0=ALU.mult)
                if t0 + tw == Ti:
                    last = e.tensor_scalar(xq, xq, halo_cm[:tw, 1:2], None,
                                           op0=ALU.mult)
                last.then_inc(s_xq.h, 1)
            s_xq.n += 1

            @blk.sync
            def _(e, t0=t0, tw=tw, ti=ti, pa0=pa0, pv0=pv0):
                e.wait_ge(s_pv.h, pv0 + 4)
                e.dma_start(bass.AP(dd["row_scr"], amax_off + t0, [[1, 1], [1, tw]]),
                            bass.AP(colB, (ti % 2) * 16 + 10, [[96, tw], [1, 1]])).then_inc(s_sp.h, 16)
                e.wait_ge(s_pa.h, pa0 + 3)
                e.dma_start(dd["x1_scr"][t0:t0 + tw, :],
                            x1t[:tw, (ti % 2) * D:(ti % 2) * D + D]).then_inc(s_sp.h, 16)
            s_sp.n += 32

            @blk.tensor
            def _(e, t0=t0, tw=tw, ti=ti, xq0=xq0):
                e.wait_ge(s_xq.h, xq0 + 1)
                xq = xqtile[:tw, (ti % 2) * D:(ti % 2) * D + D]
                for dc in range(DC):
                    mm = e.matmul(psB_s[:, (dc % 8) * 128:(dc % 8) * 128 + tw],
                                  xq[:tw, dc * 128:(dc + 1) * 128], ident[:tw, :tw],
                                  start=True, stop=True)
                mm.then_inc(s_tp.h, 1)
            s_tp.n += 1

            @blk.vector
            def _(e, t0=t0, tw=tw):
                e.wait_ge(s_tp.h, s_tp.n)
                dst = bass.AP(xqT, t0, [[DC * Ti, 128], [Ti, DC], [1, tw]])
                src = bass.AP(psB_s, 0, [[2048, 128], [128, DC], [1, tw]])
                e.tensor_copy(dst, src).then_inc(s_tpe.h, 1)
            s_tpe.n += 1

    emit_wquant(mean_pieces(D, H), dd["w1_T"], ws1_col, HW2, store_w1)
    emit_wquant(mean_pieces(H, D), dd["w2_T"], ws2_col, HW2, store_w2)
    wq_done_sp = s_spg.n

    @blk.sync
    def _(e):
        if "w1q" in DBG:
            e.wait_ge(s_spg.h, wq_done_sp)
            e.dma_start(dd["dbg_w1q"][:, :], dd["w1q_scr"][:, :]).then_inc(s_sp.h, 16)
        if "w2q" in DBG:
            e.wait_ge(s_spg.h, wq_done_sp)
            e.dma_start(dd["dbg_w2q"][:, :], dd["w2q_scr"][:, :]).then_inc(s_sp.h, 16)
    for nm_ in ("w1q", "w2q"):
        if nm_ in DBG:
            s_sp.n += 16

    @blk.sync
    def _(e):
        if "kern" in DBG:
            e.dma_start(dd["dbg_kern"][:, :], kern[:, :]).then_inc(s_sp.h, 16)
        if "x1" in DBG:
            e.wait_ge(s_sp.h, s_sp.n)
            e.dma_start(dd["dbg_x1"][:, :], dd["x1_scr"][:, :]).then_inc(s_sp.h, 16)
        if "xqT" in DBG:
            e.dma_start(dd["dbg_xqT"][:, :], xqT[:, :]).then_inc(s_sp.h, 16)
        if "kT" in DBG:
            e.dma_start(dd["dbg_kT"][:, :], kT[:, :]).then_inc(s_sp.h, 16)
        if "qT" in DBG:
            e.dma_start(dd["dbg_qT"][:, :], qT[:, :]).then_inc(s_sp.h, 16)
    for name in ("kern", "x1", "xqT", "kT", "qT"):
        if name in DBG:
            s_sp.n += 16

    barrier()
    esPB.close()
    if _stop("B"):
        return nc, dd

    esC = ExitStack()
    sbC = lambda n, sh, dt=F32: esC.enter_context(nc.sbuf_tensor(n, sh, dt))
    h_half = sbC("h_half", [128, HC * TH], BF16)
    w1strip = sbC("w1strip", [128, 2 * DC * 128], BF16)
    inv1_b = sbC("inv1_b", [128, halfW])
    s2_b = sbC("s2_b", [128, TH])
    hdq = sbC("hdq", [128, 2 * halfW], BF16)
    cvt = sbC("cvt", [128, 2 * TH])
    amaxr = sbC("amaxr", [128, TH])
    convw = sbC("convw", [128, 2 * 4])
    qy = sbC("qy", [128, 2 * TH])
    qt2 = sbC("qt2", [128, 2 * TH])
    w2tile = sbC("w2tile", [128, 2 * 512], BF16)
    x1d = sbC("x1d", [128, 2 * D])
    y2_all = sbC("y2_all", [128, NTT * D])
    o2t = sbC("o2t", [128, D])
    colC = sbC("colC", [128, 16 * 2 + 64])

    @blk.vector
    def _(e):
        e.memset(colC[:, :], 1.0)
    inv2_cols = sbC("inv2_cols", [128, NTT])
    s2cols = sbC("s2cols", [128, NTT])

    for h2 in range(2):
        psC_stack = ExitStack()
        psC = psC_stack.enter_context(nc.psum_tensor("psC_%d" % h2, [128, 3072], F32))
        tb = h2 * TH
        mi0 = s_sp.n

        @blk.sync
        def _(e, tb=tb):
            e.dma_start(inv1_b[:, :],
                        bass.AP(dd["row_scr"], amax_off + tb, [[0, 128], [1, halfW]])).then_inc(s_sp.h, 16)
        s_sp.n += 16

        @blk.vector
        def _(e, mi0=mi0):
            e.wait_ge(s_sp.h, mi0 + 16)
            e.tensor_scalar(inv1_b[:, :], inv1_b[:, :], 1e-5, 1.0 / 127.0,
                            op0=ALU.max, op1=ALU.mult)
            e.tensor_scalar(inv1_b[:, :], inv1_b[:, :], wm1_col[:, :], None,
                            op0=ALU.mult)
            e.memset(amaxr[:, :], 0.0).then_inc(s_misc.h, 1)
        s_misc.n += 1
        misc_inv1 = s_misc.n

        NHC = chunks_of(halfW, 512)
        cv_base, cg_base, ca_base = s_cv.n, s_cg.n, s_ca.n

        for cs in range(HC):
            w10, cm0 = s_wl.n, s_cmm.n

            @blk.scalar
            def _(e, cs=cs, cm0=cm0, cv_base=cv_base):
                if cs >= 2:
                    e.wait_ge(s_cv.h, cv_base + 3 * (cs - 2) + 1)
                e.dma_start(w1strip[:, (cs % 2) * DC * 128:(cs % 2 + 1) * DC * 128],
                            bass.AP(dd["w1q_scr"], cs * DC * 128 * 128,
                                    [[128, 128], [128 * 128, DC], [1, 128]])).then_inc(s_wl.h, 16)
                e.dma_start(convw[:, (cs % 2) * 4:(cs % 2) * 4 + 3],
                            dd["conv_w"][cs * 128:(cs + 1) * 128, :]).then_inc(s_wl.h, 16)
                e.dma_start(convw[:, (cs % 2) * 4 + 3:(cs % 2) * 4 + 4],
                            dd["conv_b"][cs * 128:(cs + 1) * 128, :]).then_inc(s_wl.h, 16)
            s_wl.n += 48

            @blk.tensor
            def _(e, cs=cs, h2=h2, tb=tb, w10=w10, cv_base=cv_base, NHC=NHC,
                  dvn=s_dv.n):
                e.wait_ge(s_wl.h, w10 + 16)
                if cs == 0:
                    e.wait_ge(s_misc.h, misc_inv1)
                    if h2 == 0:
                        e.wait_ge(s_tpe.h, s_tpe.n)
                        e.wait_ge(s_spg.h, wq_done_sp)
                    else:
                        e.wait_ge(s_dv.h, dvn)
                if cs >= 2:
                    e.wait_ge(s_cv.h, cv_base + 3 * (cs - 2) + 1)
                for dc in range(DC):
                    stat = w1strip[:, (cs % 2) * DC * 128 + dc * 128:
                                   (cs % 2) * DC * 128 + dc * 128 + 128]
                    for (o0, ow) in NHC:
                        mm = e.matmul(psC[:, (cs % 2) * 1536 + o0:(cs % 2) * 1536 + o0 + ow],
                                      stat, xqT[:, dc * Ti + tb + o0: dc * Ti + tb + o0 + ow],
                                      start=(dc == 0), stop=(dc == DC - 1))
                mm.then_inc(s_cmm.h, 1)
            s_cmm.n += 1

            @blk.vector
            def _(e, cs=cs, cm0=cm0, cg_base=cg_base, ca_base=ca_base):
                e.wait_ge(s_cmm.h, cm0 + 1)
                e.tensor_tensor(hdq[:, (cs % 2) * halfW:(cs % 2) * halfW + halfW],
                                psC[:, (cs % 2) * 1536:(cs % 2) * 1536 + halfW],
                                inv1_b[:, :], op=ALU.mult).then_inc(s_cv.h, 1)
                e.wait_ge(s_cg.h, cg_base + cs + 1)
                sl = hdq[:, (cs % 2) * halfW:(cs % 2) * halfW + halfW]
                ct = cvt[:, (cs % 2) * TH:(cs % 2) * TH + TH]
                e.scalar_tensor_tensor(ct, sl[:, 1:TH + 1],
                                       convw[:, (cs % 2) * 4 + 1:(cs % 2) * 4 + 2],
                                       ct, op0=ALU.mult, op1=ALU.add)
                e.scalar_tensor_tensor(ct, sl[:, 2:TH + 2],
                                       convw[:, (cs % 2) * 4 + 2:(cs % 2) * 4 + 3],
                                       ct, op0=ALU.mult, op1=ALU.add).then_inc(s_cv.h, 1)
                e.wait_ge(s_ca.h, ca_base + cs + 1)
                e.tensor_tensor(amaxr[:, :], amaxr[:, :],
                                h_half[:, cs * TH:(cs + 1) * TH], op=ALU.max)
                e.scalar_tensor_tensor(amaxr[:, :], h_half[:, cs * TH:(cs + 1) * TH],
                                       -1.0, amaxr[:, :],
                                       op0=ALU.mult, op1=ALU.max).then_inc(s_cv.h, 1)
            s_cv.n += 3

            @blk.gpsimd
            def _(e, cs=cs, w10=w10, cv_base=cv_base, ca_base=ca_base):
                e.wait_ge(s_cv.h, cv_base + 3 * cs + 1)
                e.wait_ge(s_wl.h, w10 + 48)
                if cs >= 2:
                    e.wait_ge(s_ca.h, ca_base + cs - 1)
                sl = hdq[:, (cs % 2) * halfW:(cs % 2) * halfW + halfW]
                e.tensor_scalar(cvt[:, (cs % 2) * TH:(cs % 2) * TH + TH],
                                sl[:, 0:TH], convw[:, (cs % 2) * 4:(cs % 2) * 4 + 1],
                                None, op0=ALU.mult).then_inc(s_cg.h, 1)
            s_cg.n += 1

            @blk.scalar
            def _(e, cs=cs, cv_base=cv_base):
                e.wait_ge(s_cv.h, cv_base + 3 * cs + 2)
                e.activation(h_half[:, cs * TH:(cs + 1) * TH],
                             cvt[:, (cs % 2) * TH:(cs % 2) * TH + TH],
                             AF.Silu,
                             bias=convw[:, (cs % 2) * 4 + 3:(cs % 2) * 4 + 4]).then_inc(s_ca.h, 1)
            s_ca.n += 1

        q2_0 = s_q2.n

        @blk.tensor
        def _(e):
            e.wait_ge(s_cv.h, s_cv.n)
            for tt in range(NTT):
                mm = e.matmul(psC[:, tt * 128:(tt + 1) * 128],
                              amaxr[:, tt * 128:(tt + 1) * 128], ident_f[:, :],
                              start=True, stop=True)
            mm.then_inc(s_q2.h, 1)
        s_q2.n += 1

        @blk.vector
        def _(e, q2_0=q2_0):
            e.wait_ge(s_q2.h, q2_0 + 1)
            for tt in range(NTT):
                e.reduce_max(colC[:, 8:9], psC[:, tt * 128:(tt + 1) * 128], axis=AX.X)
                dve_gap(e)
                dve_gap(e)
                e.tensor_scalar(wsA[:, :], colC[:, 8:72], 1e-5, None, op0=ALU.max)
                e.tensor_scalar(wsB[:, :], wsA[:, :], 1.0 / 127.0, None,
                                op0=ALU.mult)
                e.reciprocal(wsC[:, :], wsA[:, :])
                e.tensor_tensor(wsA[:, :], wsB[:, :], wm2_col[:, :], op=ALU.mult)
                e.tensor_scalar(wsB[:, :], wsC[:, :], 127.0, None, op0=ALU.mult)
                dve_gap(e)
                e.tensor_copy(inv2_cols[:, tt:tt + 1], wsA[:, 0:1])
                last = e.tensor_copy(s2cols[:, tt:tt + 1], wsB[:, 0:1])
            last.then_inc(s_q2.h, 1)
        s_q2.n += 1

        sp_q2 = s_sp.n

        @blk.sync
        def _(e, sp_q2=sp_q2):
            e.wait_ge(s_q2.h, s_q2.n)
            for tt in range(NTT):
                e.dma_start(bass.AP(dd["row_scr"], inv2_off + tt * 128, [[1, 1], [1, 128]]),
                            s2cols[:, tt:tt + 1]).then_inc(s_sp.h, 16)
            e.wait_ge(s_sp.h, sp_q2 + 16 * NTT)
            e.dma_start(s2_b[:, :],
                        bass.AP(dd["row_scr"], inv2_off, [[0, 128], [1, TH]])).then_inc(s_sp.h, 16)
            e.wait_ge(s_sp.h, sp_q2 + 16 * (NTT + 1))
            e.sem_inc(s_q2.h, 1)
        s_sp.n += 16 * (NTT + 1)
        s_q2.n += 1
        q2_ready = s_q2.n

        @blk.sync
        def _(e, h2=h2):
            if "h" in DBG and h2 == 1:
                e.wait_ge(s_ca.h, s_ca.n)
                e.dma_start(dd["dbg_h"][:, :], h_half[:, :]).then_inc(s_sp.h, 16)
        if "h" in DBG and h2 == 1:
            s_sp.n += 16
            h_dump_done = s_sp.n

            @blk.vector
            def _(e, h_dump_done=h_dump_done):
                e.wait_ge(s_sp.h, h_dump_done)

        qv_base, qa_base = s_cv.n, s_ca.n
        for cs in range(HC):

            @blk.vector
            def _(e, cs=cs, qv_base=qv_base, qa_base=qa_base):
                if cs == 0:
                    e.wait_ge(s_q2.h, q2_ready)
                if cs >= 2:
                    e.wait_ge(s_ca.h, qa_base + (cs - 2) + 1)
                sl = h_half[:, cs * TH:(cs + 1) * TH]
                e.tensor_tensor(qy[:, (cs % 2) * TH:(cs % 2) * TH + TH], sl,
                                s2_b[:, :], op=ALU.mult).then_inc(s_cv.h, 1)
                e.wait_ge(s_ca.h, qa_base + cs + 1)
                e.tensor_copy(h_half[:, cs * TH:(cs + 1) * TH],
                              qt2.bitcast(I32)[:, (cs % 2) * TH:(cs % 2) * TH + TH])
                e.tensor_scalar(h_half[:, cs * TH:(cs + 1) * TH],
                                h_half[:, cs * TH:(cs + 1) * TH], -128.0, 127.0,
                                op0=ALU.max, op1=ALU.min).then_inc(s_cv.h, 1)
            s_cv.n += 2

            @blk.scalar
            def _(e, cs=cs, qv_base=qv_base):
                e.wait_ge(s_cv.h, qv_base + 2 * cs + 1)
                e.activation(qt2.bitcast(I32)[:, (cs % 2) * TH:(cs % 2) * TH + TH],
                             qy[:, (cs % 2) * TH:(cs % 2) * TH + TH],
                             AF.Identity).then_inc(s_ca.h, 1)
            s_ca.n += 1

        hq_done = s_cv.n

        @blk.sync
        def _(e, h2=h2):
            if "hq" in DBG and h2 == 0:
                e.wait_ge(s_cv.h, hq_done)
                e.dma_start(dd["dbg_hq"][:, :], h_half[:, :]).then_inc(s_sp.h, 16)
        if "hq" in DBG and h2 == 0:
            s_sp.n += 16

        psC_stack.close()
        psD_stack = ExitStack()
        psD = psD_stack.enter_context(nc.psum_tensor("psD_%d" % h2, [128, NTT * 512], F32))
        for (do0, dow) in chunks_of(D, 512):
            dm0, dv0, w20 = s_dmm.n, s_dv.n, s_wl.n

            @blk.scalar
            def _(e, do0=do0, dow=dow, dm0=dm0):
                for cs in range(HC):
                    if cs >= 2:
                        e.wait_ge(s_dmm.h, dm0 + cs - 1)
                    e.dma_start(w2tile[:, (cs % 2) * 512:(cs % 2) * 512 + dow],
                                dd["w2q_scr"][cs * 128:(cs + 1) * 128,
                                              do0:do0 + dow]).then_inc(s_wl.h, 16)
            s_wl.n += 16 * HC

            @blk.tensor
            def _(e, dow=dow, w20=w20, dv0=dv0):
                e.wait_ge(s_cv.h, hq_done)
                if dv0 > 0:
                    e.wait_ge(s_dv.h, dv0)
                for cs in range(HC):
                    e.wait_ge(s_wl.h, w20 + 16 * (cs + 1))
                    mov = w2tile[:, (cs % 2) * 512:(cs % 2) * 512 + dow]
                    for tt in range(NTT):
                        mm = e.matmul(psD[:, tt * 512: tt * 512 + dow],
                                      h_half[:, cs * TH + tt * 128: cs * TH + tt * 128 + 128],
                                      mov, start=(cs == 0), stop=(cs == HC - 1))
                    mm.then_inc(s_dmm.h, 1)
            s_dmm.n += HC

            @blk.vector
            def _(e, do0=do0, dow=dow, dm0=dm0):
                e.wait_ge(s_dmm.h, dm0 + HC)
                for tt in range(NTT):
                    e.tensor_scalar(y2_all[:, tt * D + do0: tt * D + do0 + dow],
                                    psD[:, tt * 512: tt * 512 + dow],
                                    inv2_cols[:, tt:tt + 1], None,
                                    op0=ALU.mult).then_inc(s_dv.h, 1)
            s_dv.n += NTT

        dv_all = s_dv.n
        out_base = s_out.n

        for tt in range(NTT):
            x1l0, dv0t, da0t = s_sp.n, s_dv.n, s_da.n
            row0 = tb + tt * 128

            @blk.sync
            def _(e, tt=tt, row0=row0):
                e.dma_start(x1d[:, (tt % 2) * D:(tt % 2) * D + D],
                            dd["x1_scr"][1 + row0:1 + row0 + 128, :]).then_inc(s_sp.h, 16)
            s_sp.n += 16

            @blk.vector
            def _(e, tt=tt, x1l0=x1l0, da0t=da0t):
                base = (tt % 2) * 16
                yt = y2_all[:, tt * D:(tt + 1) * D]
                e.wait_ge(s_dv.h, dv_all)
                e.wait_ge(s_sp.h, x1l0 + 16)
                e.tensor_tensor(yt, yt, x1d[:, (tt % 2) * D:(tt % 2) * D + D],
                                op=ALU.add).then_inc(s_dv.h, 1)
                e.wait_ge(s_da.h, da0t + 1)
                ln_var_w64(e, colC, base, 128).then_inc(s_dv.h, 1)
                e.wait_ge(s_da.h, da0t + 2)
                ln_scale_w64(e, colC, base, 128).then_inc(s_dv.h, 1)
            s_dv.n += 3

            @blk.scalar
            def _(e, tt=tt, dv0t=dv0t, out_base=out_base):
                c = colC[:, (tt % 2) * 16:(tt % 2) * 16 + 16]
                yt = y2_all[:, tt * D:(tt + 1) * D]
                e.wait_ge(s_dv.h, dv0t + 1)
                e.activation(cvt[:, 0:D], yt, AF.Identity, accum_out=c[:, 0:1])
                e.activation(cvt[:, 0:D], yt, AF.Square,
                             accum_out=c[:, 1:2]).then_inc(s_da.h, 1)
                e.wait_ge(s_dv.h, dv0t + 2)
                e.activation(c[:, 5:6], c[:, 4:5], AF.Sqrt).then_inc(s_da.h, 1)
                e.wait_ge(s_dv.h, dv0t + 3)
                if tt >= 1:
                    e.wait_ge(s_out.h, out_base + 16 * tt)
                e.activation(o2t[:, 0:D], yt,
                             AF.Identity, scale=c[:, 8:9], bias=c[:, 9:10]).then_inc(s_da.h, 1)
            s_da.n += 3

            @blk.sync
            def _(e, tt=tt, row0=row0, da0t=da0t):
                e.wait_ge(s_da.h, da0t + 3)
                e.dma_start(dd["out"][row0:row0 + 128, :],
                            o2t[:, 0:D]).then_inc(s_out.h, 16)
            s_out.n += 16

        psD_stack.close()

    @blk.sync
    def _(e):
        e.wait_ge(s_out.h, s_out.n)

    barrier()
    esC.close()
    es.close()
    return nc, dd


def _rope_tables(Lb, S):
    import jax
    import jax.numpy as jnp
    half = S // 2
    cpu = jax.local_devices(backend="cpu")[0]
    with jax.default_device(cpu):
        pos = jnp.arange(Lb, dtype=jnp.float32)[:, None]
        inv_freq = 10000.0 ** (jnp.arange(half, dtype=jnp.float32) / half)
        sinus = pos * inv_freq[None, :]
        sin = np.asarray(jnp.sin(sinus), dtype=np.float32)
        cos = np.asarray(jnp.cos(sinus), dtype=np.float32)
    return np.ascontiguousarray(cos.T), np.ascontiguousarray(sin.T)


def make_in_maps(cfg, inputs):
    D, E, S, H, Lb, W = cfg["D"], cfg["E"], cfg["S"], cfg["H"], cfg["Lb"], cfg["WREL"]
    B = cfg["B"]
    n_cores = cfg["n_cores"]
    Town = Lb // 2
    Ti = Town + 2
    f32 = np.float32
    x = np.asarray(inputs["L0_attn_x"], f32)
    uv_w = np.asarray(inputs["L0_attn_uv_w"], f32)
    gamma = np.asarray(inputs["L0_attn_gamma"], f32)
    beta = np.asarray(inputs["L0_attn_beta"], f32)
    w_rel = np.asarray(inputs["L0_attn_w_rel"], f32)
    o_w = np.asarray(inputs["L0_attn_o_w"], f32)
    res = np.asarray(inputs["L0_attn_res_scale"], f32)
    g = np.asarray(inputs["L0_attn_g"], f32)
    w1 = np.asarray(inputs["L0_ffn_w1"], f32)
    conv_w = np.asarray(inputs["L0_ffn_conv_w"], f32)
    conv_b = np.asarray(inputs["L0_ffn_conv_b"], f32)
    w2 = np.asarray(inputs["L0_ffn_w2"], f32)

    import ml_dtypes
    shared = {
        "uvw_u_T": np.ascontiguousarray(uv_w[0:E].T),
        "uvw_v_T": np.ascontiguousarray(uv_w[E:2 * E].T),
        "uvw_b_T": np.ascontiguousarray(uv_w[2 * E:2 * E + S].T),
        "o_w_T": np.ascontiguousarray(o_w.T),
        "w1_T": np.ascontiguousarray(w1.T),
        "w2_T": np.ascontiguousarray(w2.T),
        "conv_w": np.ascontiguousarray(conv_w.reshape(H, 3)),
        "conv_b": np.ascontiguousarray(conv_b.reshape(H, 1)),
        "gammaT": np.ascontiguousarray(gamma.T),
        "betaT": np.ascontiguousarray(beta.T),
        "g_s": g.reshape(1, 1),
        "res_row": res.reshape(1, D),
        "w_rel": w_rel.reshape(1, W),
        "ident": np.eye(128, dtype=ml_dtypes.bfloat16),
        "ident_f": np.eye(128, dtype=np.float32),
    }
    cos_k, sin_k = _rope_tables(Lb, S)
    shared["cos_k"] = cos_k
    shared["sin_k"] = sin_k

    in_maps = []
    for c in range(n_cores):
        b, h = c // 2, c % 2
        g0 = h * Town
        xb = np.ascontiguousarray(x[b])
        xo = np.zeros((Ti, D), f32)
        lo = g0 - 1
        src_lo = max(lo, 0)
        src_hi = min(g0 + Town + 1, Lb)
        xo[src_lo - lo: src_hi - lo] = xb[src_lo:src_hi]
        cq = np.zeros((S // 2, Ti), f32)
        sq = np.zeros((S // 2, Ti), f32)
        cq[:, src_lo - lo: src_hi - lo] = cos_k[:, src_lo:src_hi]
        sq[:, src_lo - lo: src_hi - lo] = sin_k[:, src_lo:src_hi]
        m_ext = np.full((1, W + 1), 1.0 if h == 1 else 0.0, f32)
        maskL = 1.0 if h == 1 else 0.0
        maskR = 1.0 if h == 0 else 0.0
        halo_m = np.array([[maskL, maskR]], f32)
        halo_cm = np.ones((128, 2), f32)
        halo_cm[0, 0] = maskL
        halo_cm[(Ti - 1) % 128 if Ti % 128 else 127, 1] = maskR
        im = dict(shared)
        im.update({
            "x_batch": xb, "x_own": xo, "cos_q": cq, "sin_q": sq,
            "m_ext": m_ext, "halo_m": halo_m, "halo_cm": halo_cm,
        })
        in_maps.append(im)
    return in_maps


_PROG_CACHE = {}


def kernel(**inputs):
    cfg = FULL_CFG
    key = "full"
    if key not in _PROG_CACHE:
        _PROG_CACHE[key] = build_program(cfg)
    nc, dd = _PROG_CACHE[key]
    in_maps = make_in_maps(cfg, inputs)
    res = run_bass_kernel_spmd(nc, in_maps, core_ids=list(range(cfg["n_cores"])))
    B, Lb, D = cfg["B"], cfg["Lb"], cfg["D"]
    Town = Lb // 2
    out = np.empty((B, Lb, D), np.float32)
    for c in range(cfg["n_cores"]):
        b, h = c // 2, c % 2
        out[b, h * Town:(h + 1) * Town] = res.results[c]["out"]
    return out
